# version 16
# baseline (speedup 1.0000x reference)
import numpy as np
from contextlib import ExitStack

import concourse.bass as bass
import concourse.mybir as mybir
from concourse.bass_utils import run_bass_kernel_spmd

F32 = mybir.dt.float32
BF16 = mybir.dt.bfloat16
FP8 = mybir.dt.float8e4
AF = mybir.ActivationFunctionType
OP = mybir.AluOpType
AX = mybir.AxisListType.X

P = 128
D = 2048
DC = 16
T = 2048
TH = 1024
NH = 2
E = 8192
NET = 64
EBT = 16
NBLK = NET // EBT
NST = 16
TCF = 512
TSM = 256
NSUB = TH // TSM
DX_T = 16
CG = 8
WEAVE_RATIO = 5.0

ENGINES = ("sync", "gpsimd", "scalar", "vector", "tensor")
DMA_SEMS = ("WU0", "WU1", "WG0", "WG1", "WD0", "WD1", "XX0", "XX1",
            "WF0", "WF1", "QS0", "QS1", "QS2", "QS3", "XC0", "XC1", "SC0", "SC1",
            "XL", "XS", "XM", "XW", "XO", "BC", "AD",
            "XS0", "XS1", "XS2", "XS3", "XL0", "XL1", "XL2", "XL3",
            "XO0", "XO1", "XO2", "XO3")
CMP_SEMS = ("T", "V", "S", "G")


class Prog:
    def __init__(self, sems):
        self.ev = {e: [] for e in ENGINES}
        self.sems = sems
        self.cnt = {k: 0 for k in sems}

    def op(self, eng, meth, *args, inc=None, **kwargs):
        tgt = None
        h = None
        if inc is not None:
            sem, n = inc
            self.cnt[sem] += n
            tgt = self.cnt[sem]
            h = (self.sems[sem], n)
        self.ev[eng].append((meth, args, kwargs, h))
        return tgt

    def wait(self, eng, sem, val):
        if val is None or val <= 0:
            return
        assert val <= self.cnt[sem], f"wait {sem}>={val} but only {self.cnt[sem]}"
        if sem in DMA_SEMS:
            assert val == self.cnt[sem], \
                f"DMA sem {sem} wait {val} != full {self.cnt[sem]}"
        self.ev[eng].append(("wait_ge", (self.sems[sem], val), {}, None))

    def dwait(self, eng, sem):
        self.wait(eng, sem, self.cnt[sem])


def host_layout(x, A_log, Dv, W_xproj, W_up, W_gate, W_down):
    c = np.ascontiguousarray
    xs = [c(np.asarray(x[b]).T) for b in range(8)]
    WuT = np.asarray(W_up).T
    WgT = np.asarray(W_gate).T
    WdT = np.asarray(W_down).T
    WxT = np.asarray(W_xproj).T
    xu = c(WuT.reshape(DC, P, NET, P).transpose(2, 1, 0, 3)).reshape(NET, P, DC * P)
    xg = c(WgT.reshape(DC, P, NET, P).transpose(2, 1, 0, 3)).reshape(NET, P, DC * P)
    xd = c(WdT.reshape(NET, P, DC, P).transpose(2, 1, 0, 3)).reshape(DC, P, NET * P)
    xx = c(WxT[:, :D].reshape(DC, P, DX_T, P).transpose(2, 1, 0, 3)).reshape(DX_T, P, DC * P)
    xbc = c(WxT[:, D:].reshape(DC, P, 32).transpose(1, 0, 2)).reshape(P, DC * 32)
    a_re = c(np.asarray(A_log).reshape(DC, P, NST).transpose(1, 0, 2)).reshape(P, DC * NST)
    d_re = c(np.asarray(Dv).reshape(DC, P).T)
    return xs, dict(xu=xu, xg=xg, xd=xd, xx=xx, xbc=xbc, a_re=a_re, d_re=d_re)


def build_nc(stage="full", debug=False):
    nc = bass.Bass("TRN2", target_bir_lowering=False)

    x_d = nc.dram_tensor("x_fm", [D, T], F32, kind="ExternalInput").ap()
    xu_d = nc.dram_tensor("xu", [NET, P, DC * P], F32, kind="ExternalInput").ap()
    xg_d = nc.dram_tensor("xg", [NET, P, DC * P], F32, kind="ExternalInput").ap()
    xd_d = nc.dram_tensor("xd", [DC, P, NET * P], F32, kind="ExternalInput").ap()
    xx_d = nc.dram_tensor("xx", [DX_T, P, DC * P], F32, kind="ExternalInput").ap()
    xbc_d = nc.dram_tensor("xbc", [P, DC * 32], F32, kind="ExternalInput").ap()
    a_d = nc.dram_tensor("a_re", [P, DC * NST], F32, kind="ExternalInput").ap()
    dD_d = nc.dram_tensor("d_re", [P, DC], F32, kind="ExternalInput").ap()
    out_d = nc.dram_tensor("out_fm", [D, T], F32, kind="ExternalOutput").ap()
    squ_d = nc.dram_tensor("squ", [NET, P, DC * P], FP8, kind="Internal").ap()
    sqg_d = nc.dram_tensor("sqg", [NET, P, DC * P], FP8, kind="Internal").ap()
    sqd_d = nc.dram_tensor("sqd", [DC, P, NET * P], FP8, kind="Internal").ap()
    sxx_d = nc.dram_tensor("sxx", [DX_T, P, DC * P], BF16, kind="Internal").ap()
    sxbc_d = nc.dram_tensor("sxbc", [P, DC * 32], BF16, kind="Internal").ap()
    xs_d = nc.dram_tensor("xs", [D, T], F32,
                          kind="ExternalOutput" if debug else "Internal").ap()
    dbg_d = (nc.dram_tensor("dbg", [D, T], F32, kind="ExternalOutput").ap()
             if debug else None)
    dbg2_d = (nc.dram_tensor("dbg2", [P, 16], F32, kind="ExternalOutput").ap()
              if debug else None)
    dbg3_d = (nc.dram_tensor("dbg3", [1, 16], F32, kind="ExternalOutput").ap()
              if debug else None)
    if debug:
        squ_d = nc.dram_tensor("squ2", [NET, P, DC * P], FP8, kind="ExternalOutput").ap()

    es = ExitStack()
    sb = lambda name, shape, dt: es.enter_context(nc.sbuf_tensor(name, shape, dt))
    with es:
        x_sb = sb("x_sb", [P, DC * TH], F32)
        xln_sb = sb("xln_sb", [P, DC * TH], BF16)
        far = sb("far", [P, 4608], F32)
        hid = far[:, 0:4096].bitcast(BF16)
        silu_v = far[:, 4096:4608].bitcast(BF16)
        dt_sb = sb("dt_sb", [P, CG * TSM], F32)
        dtx_sb = sb("dtx_sb", [P, CG * TSM], F32)
        xm_sb = sb("xm_sb", [P, DC * TSM], BF16)
        xmf_sb = sb("xmf_sb", [P, CG * TSM], F32)
        scn_sb = sb("scn_sb", [P, 4096], F32)
        ab_v = [scn_sb[:, k * TSM:(k + 1) * TSM] for k in range(4)]
        u_v = [scn_sb[:, (4 + k) * TSM:(5 + k) * TSM] for k in range(4)]
        h_v = [scn_sb[:, (8 + k) * TSM:(8 + k) * TSM + TSM // 2].bitcast(BF16)
               for k in range(4)]
        w_vb = [scn_sb[:, (12 + k) * TSM:(12 + k) * TSM + TSM // 2].bitcast(BF16)
                for k in range(4)]
        w_v = [scn_sb[:, (12 + k) * TSM:(13 + k) * TSM] for k in range(4)]
        q_wf = [dt_sb[:, 0:2048], dtx_sb[:, 0:2048]]
        q_qa = [xm_sb[:, 0:2048], xm_sb[:, 2048:4096]]
        q_q8 = [xmf_sb[:, 0:512].bitcast(FP8), xmf_sb[:, 512:1024].bitcast(FP8)]
        red_v = scn_sb[:, 0:200]
        wu_sb = sb("wu_sb", [P, 2 * DC * P], FP8)
        wg_sb = sb("wg_sb", [P, 2 * DC * P], FP8)
        wd_sb = sb("wd_sb", [P, 2 * EBT * P], FP8)
        xxw_sb = sb("xxw_sb", [P, 2 * DC * P], BF16)
        bbc_sb = sb("bbc_sb", [P, 4 * TSM], F32)
        cbc_sb = sb("cbc_sb", [P, 4 * TSM], F32)
        bc_sb = sb("bc_sb", [32, TSM], F32)
        bcr_sb = sb("bcr_sb", [1, 2 * TSM], F32)
        st_sb = sb("st_sb", [P, 256], F32)
        a_sb = sb("a_sb", [P, DC * NST], F32)
        dD_sb = sb("dD_sb", [P, DC], F32)
        gev_sb = sb("gev_sb", [P, 2 * TCF], BF16)
        sqL = sb("sqL", [P, 4 * TSM], BF16)
        rowsL = sb("rowsL", [1, 2 * TSM], F32)
        mbcL = sb("mbcL", [P, TSM], F32)
        rbcL = sb("rbcL", [P, TSM], F32)
        tmpL = sb("tmpL", [P, 2 * TSM], F32)
        ones32 = sb("ones32", [32, P], F32)
        onesf = sb("onesf", [P, 1], F32)
        onesb = sb("onesb", [P, 1], BF16)
        cst = sb("cst", [P, 3], F32)
        sc_sb = sb("sc_sb", [P, 16], F32)
        srow = sb("srow", [1, 16], F32)
        nc.const_aps.register(F32, 1.0, cst[:, 0:1])
        nc.const_aps.register(F32, 1e-5, cst[:, 1:2])
        nc.const_aps.register(F32, -1.0, cst[:, 2:3])

        ps = [es.enter_context(nc.psum_tensor(f"ps{i}", [P, 512], F32)) for i in range(8)]
        semh = {k: es.enter_context(nc.semaphore(k)) for k in DMA_SEMS + CMP_SEMS}
        b = Prog(semh)

        SU, SG, GD, THU, THG, THD, THUN, THGN, THDN, GDS = range(10)
        sc = lambda j: sc_sb[:, j:j + 1]
        bank_free = [("T", 0)] * 8

        def mm_group(bank, pairs, out_ap=None, wait_rhs=None):
            s, v = bank_free[bank]
            b.wait("tensor", s, v)
            if wait_rhs is not None:
                b.wait("tensor", wait_rhs[0], wait_rhs[1])
            o = out_ap if out_ap is not None else ps[bank][:]
            n = len(pairs)
            for i, (l, r) in enumerate(pairs):
                last = i == n - 1
                b.op("tensor", "matmul", o, l, r, start=(i == 0), stop=last,
                     inc=(("T", 1) if last else None))
            return b.cnt["T"]

        def re3(ap_sb, ngrp):
            return ap_sb.rearrange("p (c t) -> p c t", c=ngrp)

        def dram3(rows0, nrows, c0, w):
            return xs_d[rows0:rows0 + nrows, c0:c0 + w].rearrange(
                "(c p) t -> p c t", p=P)

        b.op("vector", "memset", cst[:, 0:1], 1.0)
        b.op("vector", "memset", cst[:, 1:2], 1e-5)
        b.op("vector", "memset", cst[:, 2:3], -1.0)
        b.op("vector", "memset", ones32[:], 1.0)
        b.op("vector", "memset", onesf[:], 1.0)
        b.op("vector", "memset", onesb[:], 1.0)
        b.op("vector", "memset", st_sb[:], 0.0, inc=("V", 1))
        b.op("sync", "dma_start", out=a_sb[:], in_=a_d[:], inc=("AD", 16))
        b.op("sync", "dma_start", out=dD_sb[:], in_=dD_d[:], inc=("AD", 16))
        b.dwait("scalar", "AD")
        b.op("scalar", "activation", out=a_sb[:], in_=a_sb[:], func=AF.Exp, inc=("S", 1))
        b.wait("vector", "S", b.cnt["S"])
        b.op("vector", "tensor_scalar_mul", a_sb[:], a_sb[:], -1.0, inc=("V", 1))

        b.op("scalar", "dma_start", out=re3(x_sb[:], DC),
             in_=x_d[:, 0:TH].rearrange("(c p) t -> p c t", p=P), inc=("XL", 16))
        for et in range(DX_T):
            buf = et % 2
            scs = "SC0" if buf == 0 else "SC1"
            xcs = "XC0" if buf == 0 else "XC1"
            xv = xxw_sb[:, buf * DC * P:(buf + 1) * DC * P]
            b.dwait("gpsimd", scs)
            b.op("gpsimd", "dma_start", out=xv, in_=xx_d[et], inc=(xcs, 16))
            b.dwait("scalar", xcs)
            b.op("scalar", "dma_start", out=sxx_d[et], in_=xv, inc=(scs, 16))
        b.dwait("gpsimd", "SC0")
        xbcv = xxw_sb[:, 0:DC * 32]
        b.op("gpsimd", "dma_start", out=xbcv, in_=xbc_d[:], inc=("XC0", 16))
        b.dwait("scalar", "XC0")
        b.op("scalar", "dma_start", out=sxbc_d[:], in_=xbcv, inc=("SC0", 16))
        xxw_free = [("SC0", None), ("SC1", None)]

        wf_free = [("V", 0), ("V", 0)]

        def scale_tile(i, src_ap, queue):
            buf = i % 2
            wfs = "WF0" if buf == 0 else "WF1"
            s, v = wf_free[buf]
            b.wait(queue, s, v)
            b.op(queue, "dma_start", out=q_wf[buf], in_=src_ap, inc=(wfs, 16))
            b.dwait("vector", wfs)
            ci = b.op("vector", "reduce_sum", out=red_v[:, i:i + 1], in_=q_wf[buf],
                      axis=AX, apply_absolute_value=True, inc=("V", 1))
            wf_free[buf] = ("V", ci)

        def finalize_scale(m):
            b.op("vector", "reduce_sum", out=red_v[:, 192 + m:193 + m],
                 in_=red_v[:, m * 64:(m + 1) * 64], axis=AX, inc=("V", 1))
            vfin = b.cnt["V"]
            tcnt = mm_group(6, [(onesf[:], red_v[:, 192 + m:193 + m])],
                            out_ap=ps[6][0:1, 0:1], wait_rhs=("V", vfin))
            b.wait("vector", "T", tcnt)
            c0 = 1.0 / 16777216.0
            mulmap = {0: -1.0, 1: -1.0, 2: -0.5}
            b.op("vector", "tensor_scalar", out=srow[0:1, m:m + 1],
                 in0=ps[6][0:1, 0:1], scalar1=c0, scalar2=1e-8,
                 op0=OP.mult, op1=OP.add)
            b.op("vector", "tensor_scalar", out=srow[0:1, 3 + m:4 + m],
                 in0=ps[6][0:1, 0:1], scalar1=mulmap[m] * c0,
                 scalar2=mulmap[m] * 1e-8, op0=OP.mult, op1=OP.add)
            b.op("vector", "tensor_scalar", out=srow[0:1, 6 + m:7 + m],
                 in0=ps[6][0:1, 0:1], scalar1=0.5 * c0, scalar2=0.5e-8,
                 op0=OP.mult, op1=OP.add)
            b.op("vector", "tensor_scalar", out=srow[0:1, 9 + m:10 + m],
                 in0=ps[6][0:1, 0:1], scalar1=-0.5 * c0, scalar2=-0.5e-8,
                 op0=OP.mult, op1=OP.add, inc=("V", 1))
            bank_free[6] = ("V", b.cnt["V"])
            vscal = b.cnt["V"]
            for j, sr in [((SU, 3), (THU, 6), (THUN, 9)),
                          ((SG, 4), (THG, 7), (THGN, 10)),
                          ((GD, 5), (THD, 8), (THDN, 11))][m]:
                tcnt = mm_group(6, [(ones32[0:1, :], srow[0:1, sr:sr + 1])],
                                out_ap=ps[6][:, 0:1], wait_rhs=("V", vscal))
                b.wait("vector", "T", tcnt)
                b.op("vector", "tensor_copy", out=sc(j), in_=ps[6][:, 0:1],
                     inc=("V", 1))
                if j == GD:
                    b.op("vector", "tensor_mul", sc(GDS), ps[6][:, 0:1], sc(SG),
                         inc=("V", 1))
                bank_free[6] = ("V", b.cnt["V"])

        for et in range(NET):
            scale_tile(et, xu_d[et], "sync")
        for et in range(NET):
            scale_tile(NET + et, xg_d[et], "sync")
        finalize_scale(0)
        finalize_scale(1)

        q2 = []
        for r in range(32):
            q2 += [(xu_d[2 * r], squ_d[2 * r], 0, ("u", 2 * r)),
                   (xu_d[2 * r + 1], squ_d[2 * r + 1], 0, ("u", 2 * r + 1)),
                   (xg_d[2 * r], sqg_d[2 * r], 1, ("g", 2 * r)),
                   (xg_d[2 * r + 1], sqg_d[2 * r + 1], 1, ("g", 2 * r + 1))]
            s4, dt0 = r // 8, 2 * (r % 8)
            for dt_i in (dt0, dt0 + 1):
                sl = slice(s4 * 2048, (s4 + 1) * 2048)
                q2.append((xd_d[dt_i][:, sl], sqd_d[dt_i][:, sl], 2,
                           ("d", dt_i, s4)))
        qmark = {}
        qa_free = [("V", 0), ("V", 0)]
        qs_names = ("QS0", "QS1", "QS2", "QS3")
        q2_src = [None]

        def need_q(key):
            while key not in qmark and q2_src[0] is not None:
                try:
                    next(q2_src[0])
                except StopIteration:
                    q2_src[0] = None
            b.wait("sync", "G", qmark[key])

        def q2_gen():
            for k in range(64):
                dt_i, s4 = k // 4, k % 4
                scale_tile(128 + k,
                           xd_d[dt_i][:, s4 * 2048:(s4 + 1) * 2048], "gpsimd")
                yield
            finalize_scale(2)
            pend = []
            for i, (src_ap, dst, mat, key) in enumerate(q2):
                buf = i % 2
                wfs = "WF0" if buf == 0 else "WF1"
                qss = qs_names[i % 4]
                s, v = wf_free[buf]
                b.wait("gpsimd", s, v)
                b.op("gpsimd", "dma_start", out=q_wf[buf], in_=src_ap, inc=(wfs, 16))
                b.dwait("vector", wfs)
                s, v = qa_free[buf]
                b.wait("vector", s, v)
                b.op("vector", "tensor_single_scalar", out=q_qa[buf], in_=q_wf[buf],
                     scalar=sc(THU + mat), op=OP.is_gt)
                b.dwait("vector", qss)
                vi = b.op("vector", "scalar_tensor_tensor", out=q_q8[buf],
                          in0=q_wf[buf], scalar=sc(THUN + mat), in1=q_qa[buf],
                          op0=OP.is_lt, op1=OP.subtract, inc=("V", 1))
                wf_free[buf] = ("V", vi)
                qa_free[buf] = ("V", vi)
                b.wait("scalar", "V", vi)
                b.op("scalar", "dma_start", out=dst, in_=q_q8[buf], inc=(qss, 16))
                pend.append((key, qss))
                if len(pend) > 2:
                    k2, q2s = pend.pop(0)
                    b.dwait("gpsimd", q2s)
                    qmark[k2] = b.op("gpsimd", "engine_nop", inc=("G", 1))
                yield
            while pend:
                k2, q2s = pend.pop(0)
                b.dwait("gpsimd", q2s)
                qmark[k2] = b.op("gpsimd", "engine_nop", inc=("G", 1))

        xxw_free = [("SC0", b.cnt["SC0"]), ("SC1", b.cnt["SC1"])]

        ln_guard_T = [0]
        def ln_sub(s_i, after_counts=None):
            t0 = s_i * TSM
            b.wait("scalar", "V", b.cnt["V"])
            b.wait("scalar", "T", ln_guard_T[0])
            for c_i in range(DC):
                if c_i % 4 == 0:
                    b.dwait("scalar", f"XL{c_i // 4}")
                b.op("scalar", "activation", out=xm_sb[:, c_i * TSM:(c_i + 1) * TSM],
                     in_=x_sb[:, c_i * TH + t0: c_i * TH + t0 + TSM], func=AF.Copy,
                     inc=(("S", 1) if c_i == DC - 1 else None))
            tmarks = []
            for c_i in range(DC):
                cb = c_i % 4
                xbv = xm_sb[:, c_i * TSM:(c_i + 1) * TSM]
                sqv = sqL[:, cb * TSM:(cb + 1) * TSM]
                if c_i >= 4:
                    b.wait("scalar", "T", tmarks[c_i - 4])
                s_c = b.op("scalar", "activation", out=sqv, in_=xbv, func=AF.Square,
                           inc=("S", 1))
                if c_i == 0:
                    for k in (0, 1):
                        s, v = bank_free[k]
                        b.wait("tensor", s, v)
                b.wait("tensor", "S", s_c)
                b.op("tensor", "matmul", ps[0][0:1, 0:TSM], onesb[:], xbv,
                     start=(c_i == 0), stop=(c_i == DC - 1))
                b.op("tensor", "matmul", ps[1][0:1, 0:TSM], onesb[:], sqv,
                     start=(c_i == 0), stop=(c_i == DC - 1), inc=("T", 1))
                tmarks.append(b.cnt["T"])
            tstat = b.cnt["T"]
            ln_guard_T[0] = tstat
            b.wait("vector", "T", tstat)
            b.op("vector", "tensor_scalar_mul", rowsL[0:1, 0:TSM],
                 ps[0][0:1, 0:TSM], 1.0 / D)
            b.op("vector", "tensor_scalar_mul", rowsL[0:1, TSM:2 * TSM],
                 ps[1][0:1, 0:TSM], 1.0 / D, inc=("V", 1))
            bank_free[0] = ("V", b.cnt["V"])
            bank_free[1] = ("V", b.cnt["V"])
            b.op("vector", "scalar_tensor_tensor", out=tmpL[0:1, 0:TSM],
                 in0=rowsL[0:1, 0:TSM], scalar=-1.0, in1=rowsL[0:1, 0:TSM],
                 op0=OP.mult, op1=OP.mult)
            b.op("vector", "tensor_add", rowsL[0:1, TSM:2 * TSM],
                 rowsL[0:1, TSM:2 * TSM], tmpL[0:1, 0:TSM], inc=("V", 1))
            b.wait("scalar", "V", b.cnt["V"])
            b.op("scalar", "activation", out=rowsL[0:1, TSM:2 * TSM],
                 in_=rowsL[0:1, TSM:2 * TSM], func=AF.Sqrt, bias=1e-5, inc=("S", 1))
            b.wait("vector", "S", b.cnt["S"])
            b.op("vector", "reciprocal", out=rowsL[0:1, TSM:2 * TSM],
                 in_=rowsL[0:1, TSM:2 * TSM])
            b.op("vector", "tensor_mul", rowsL[0:1, 0:TSM], rowsL[0:1, 0:TSM],
                 rowsL[0:1, TSM:2 * TSM], inc=("V", 1))
            vrows = b.cnt["V"]
            t_r = mm_group(2, [(ones32[0:1, :], rowsL[0:1, TSM:2 * TSM])],
                           out_ap=ps[2][:, 0:TSM], wait_rhs=("V", vrows))
            t_m = mm_group(3, [(ones32[0:1, :], rowsL[0:1, 0:TSM])],
                           out_ap=ps[3][:, 0:TSM])
            b.wait("vector", "T", t_m)
            b.op("vector", "tensor_copy", out=rbcL[:], in_=ps[2][:, 0:TSM])
            b.op("vector", "tensor_copy", out=mbcL[:], in_=ps[3][:, 0:TSM],
                 inc=("V", 1))
            bank_free[2] = ("V", b.cnt["V"])
            bank_free[3] = ("V", b.cnt["V"])
            for c_i in range(DC):
                xsl = x_sb[:, c_i * TH + t0: c_i * TH + t0 + TSM]
                tv = tmpL[:, (c_i % 2) * TSM:(c_i % 2 + 1) * TSM]
                b.op("vector", "tensor_mul", tv, xsl, rbcL[:])
                b.op("vector", "tensor_sub",
                     xln_sb[:, c_i * TH + t0: c_i * TH + t0 + TSM], tv, mbcL[:],
                     inc=("V", 1))
            return b.cnt["V"]

        wu_free = [("T", 0), ("T", 0)]
        wg_free = [("T", 0), ("T", 0)]
        wd_free = [("T", 0), ("T", 0)]
        hid_guard = [("T", 0)]
        silu_guard = [("V", 0), ("V", 0)]
        gev_guard = [("V", 0), ("V", 0)]

        def ffn_gen(xln_ready, jit=False):
            if not jit:
                b.wait("sync", "G", qmark.get(("d", DC - 1, NBLK - 1)))
            b.wait("tensor", "V", xln_ready)
            for tc in range(2):
                for blk in range(NBLK):
                    prev_guard = hid_guard[0]
                    for j in range(EBT):
                        et = blk * EBT + j
                        buf = j % 2
                        wus = "WU0" if buf == 0 else "WU1"
                        wgs = "WG0" if buf == 0 else "WG1"
                        wuv = wu_sb[:, buf * DC * P:(buf + 1) * DC * P]
                        wgv = wg_sb[:, buf * DC * P:(buf + 1) * DC * P]
                        s, v = wu_free[buf]
                        b.wait("sync", s, v)
                        if jit:
                            need_q(("u", et))
                        b.op("sync", "dma_start", out=wuv, in_=squ_d[et], inc=(wus, 16))
                        s, v = wg_free[buf]
                        b.wait("sync", s, v)
                        if jit:
                            need_q(("g", et))
                        b.op("sync", "dma_start", out=wgv, in_=sqg_d[et], inc=(wgs, 16))
                        pu, pg = buf, 2 + buf
                        xsl = [xln_sb[:, c_i * TH + tc * TCF: c_i * TH + (tc + 1) * TCF]
                               for c_i in range(DC)]
                        b.dwait("tensor", wus)
                        tu = mm_group(pu, [(wuv[:, c_i * P:(c_i + 1) * P], xsl[c_i])
                                           for c_i in range(DC)])
                        b.dwait("tensor", wgs)
                        tg = mm_group(pg, [(wgv[:, c_i * P:(c_i + 1) * P], xsl[c_i])
                                           for c_i in range(DC)])
                        wu_free[buf] = ("T", tu)
                        wg_free[buf] = ("T", tg)
                        b.wait("scalar", "T", tu)
                        s, v = silu_guard[buf]
                        b.wait("scalar", s, v)
                        s_i2 = b.op("scalar", "activation",
                                    out=silu_v[:, buf * TCF:(buf + 1) * TCF],
                                    in_=ps[pu][:], func=AF.Silu, scale=sc(SU),
                                    inc=("S", 1))
                        bank_free[pu] = ("S", s_i2)
                        b.wait("scalar", "T", tg)
                        s, v = gev_guard[buf]
                        b.wait("scalar", s, v)
                        s_g = b.op("scalar", "activation",
                                   out=gev_sb[:, buf * TCF:(buf + 1) * TCF],
                                   in_=ps[pg][:], func=AF.Copy, inc=("S", 1))
                        bank_free[pg] = ("S", s_g)
                        b.wait("vector", "S", s_g)
                        if j <= 1:
                            s, v = prev_guard
                            b.wait("vector", s, v)
                        gi = b.op("vector", "tensor_mul",
                                  hid[:, j * TCF:(j + 1) * TCF],
                                  gev_sb[:, buf * TCF:(buf + 1) * TCF],
                                  silu_v[:, buf * TCF:(buf + 1) * TCF],
                                  inc=("V", 1))
                        gev_guard[buf] = ("V", gi)
                        silu_guard[buf] = ("V", gi)
                        yield
                    hid_ready = b.cnt["V"]
                    for dt_i in range(DC):
                        buf = dt_i % 2
                        wds = "WD0" if buf == 0 else "WD1"
                        wdv = wd_sb[:, buf * EBT * P:(buf + 1) * EBT * P]
                        s, v = wd_free[buf]
                        b.wait("sync", s, v)
                        if jit:
                            need_q(("d", dt_i, blk))
                        b.op("sync", "dma_start", out=wdv,
                             in_=sqd_d[dt_i][:, blk * EBT * P:(blk + 1) * EBT * P],
                             inc=(wds, 16))
                        pd = buf
                        b.dwait("tensor", wds)
                        td = mm_group(pd, [(wdv[:, ce * P:(ce + 1) * P],
                                            hid[:, ce * TCF:(ce + 1) * TCF])
                                           for ce in range(EBT)],
                                      wait_rhs=(("V", hid_ready) if dt_i == 0 else None))
                        wd_free[buf] = ("T", td)
                        xout = x_sb[:, dt_i * TH + tc * TCF: dt_i * TH + (tc + 1) * TCF]
                        b.wait("vector", "T", td)
                        vi = b.op("vector", "scalar_tensor_tensor", out=xout,
                                  in0=ps[pd][:], scalar=sc(GDS), in1=xout,
                                  op0=OP.mult, op1=OP.add, inc=("V", 1))
                        bank_free[pd] = ("V", vi)
                        yield
                    hid_guard[0] = ("T", b.cnt["T"])

        scan_state = {"pos": 0, "ab": {}, "u": {}, "hs": {}, "w": {},
                      "vh": {}, "vw": {}, "pend_st": [], "pend_add": []}
        bcr_free = [("T", 0)]
        xm_free = [("V", 0)]
        dt_guard = [0]

        def mx_load_group(h, s_i, g):
            t0 = h * TH + s_i * TSM
            s, v = xm_free[0]
            b.wait("gpsimd", s, v)
            b.dwait("gpsimd", "XW")
            for q in range(4):
                b.dwait("gpsimd", f"XS{q}")
            b.dwait("gpsimd", "QS0")
            b.dwait("gpsimd", "QS1")
            b.op("gpsimd", "dma_start", out=re3(xmf_sb[:], CG),
                 in_=dram3(g * CG * P, CG * P, t0, TSM), inc=("XM", 16))

        def mixer_gen(h):
            ss = scan_state
            for s_i in range(NSUB):
                t0g = h * TH + s_i * TSM
                ln_tm = []
                for g in range(2):
                    mx_load_group(h, s_i, g)
                    b.dwait("scalar", "XM")
                    if g == 0:
                        b.wait("scalar", "T", ln_guard_T[0])
                    for ci in range(CG):
                        c_g = g * CG + ci
                        b.op("scalar", "activation",
                             out=xm_sb[:, c_g * TSM:(c_g + 1) * TSM],
                             in_=xmf_sb[:, ci * TSM:(ci + 1) * TSM], func=AF.Copy,
                             inc=(("S", 1) if ci == CG - 1 else None))
                    for ci in range(CG):
                        c_g = g * CG + ci
                        cb = c_g % 4
                        xbv = xm_sb[:, c_g * TSM:(c_g + 1) * TSM]
                        sqv = sqL[:, cb * TSM:(cb + 1) * TSM]
                        if c_g >= 4:
                            b.wait("scalar", "T", ln_tm[c_g - 4])
                        s_c = b.op("scalar", "activation", out=sqv, in_=xbv,
                                   func=AF.Square, inc=("S", 1))
                        if c_g == 0:
                            for k in (6, 7):
                                s, v = bank_free[k]
                                b.wait("tensor", s, v)
                        b.wait("tensor", "S", s_c)
                        b.op("tensor", "matmul", ps[6][0:1, 0:TSM], onesb[:], xbv,
                             start=(c_g == 0), stop=(c_g == DC - 1))
                        b.op("tensor", "matmul", ps[7][0:1, 0:TSM], onesb[:], sqv,
                             start=(c_g == 0), stop=(c_g == DC - 1), inc=("T", 1))
                        ln_tm.append(b.cnt["T"])
                        yield
                    xm_free[0] = ("S", b.cnt["S"])
                tstat = b.cnt["T"]
                ln_guard_T[0] = tstat
                b.wait("vector", "T", tstat)
                b.op("vector", "tensor_scalar_mul", rowsL[0:1, 0:TSM],
                     ps[6][0:1, 0:TSM], 1.0 / D)
                b.op("vector", "tensor_scalar_mul", rowsL[0:1, TSM:2 * TSM],
                     ps[7][0:1, 0:TSM], 1.0 / D, inc=("V", 1))
                bank_free[6] = ("V", b.cnt["V"])
                bank_free[7] = ("V", b.cnt["V"])
                b.op("vector", "scalar_tensor_tensor", out=tmpL[0:1, 0:TSM],
                     in0=rowsL[0:1, 0:TSM], scalar=-1.0, in1=rowsL[0:1, 0:TSM],
                     op0=OP.mult, op1=OP.mult)
                b.op("vector", "tensor_add", rowsL[0:1, TSM:2 * TSM],
                     rowsL[0:1, TSM:2 * TSM], tmpL[0:1, 0:TSM], inc=("V", 1))
                b.wait("scalar", "V", b.cnt["V"])
                b.op("scalar", "activation", out=rowsL[0:1, TSM:2 * TSM],
                     in_=rowsL[0:1, TSM:2 * TSM], func=AF.Sqrt, bias=1e-5,
                     inc=("S", 1))
                b.wait("vector", "S", b.cnt["S"])
                b.op("vector", "reciprocal", out=rowsL[0:1, TSM:2 * TSM],
                     in_=rowsL[0:1, TSM:2 * TSM])
                b.op("vector", "tensor_mul", rowsL[0:1, 0:TSM], rowsL[0:1, 0:TSM],
                     rowsL[0:1, TSM:2 * TSM], inc=("V", 1))
                vrows = b.cnt["V"]
                t_r = mm_group(6, [(ones32[0:1, :], rowsL[0:1, TSM:2 * TSM])],
                               out_ap=ps[6][:, 0:TSM], wait_rhs=("V", vrows))
                t_m = mm_group(7, [(ones32[0:1, :], rowsL[0:1, 0:TSM])],
                               out_ap=ps[7][:, 0:TSM])
                b.wait("vector", "T", t_m)
                b.op("vector", "tensor_copy", out=rbcL[:], in_=ps[6][:, 0:TSM])
                b.op("vector", "tensor_copy", out=mbcL[:], in_=ps[7][:, 0:TSM],
                     inc=("V", 1))
                bank_free[6] = ("V", b.cnt["V"])
                bank_free[7] = ("V", b.cnt["V"])
                yield
                b.wait("vector", "G", b.cnt["G"])
                for rep in range(2):
                    g = 1 - rep
                    if rep == 1:
                        xm_free[0] = ("V", b.cnt["V"])
                        mx_load_group(h, s_i, 0)
                        b.dwait("vector", "XM")
                    for ci in range(CG):
                        c_g = g * CG + ci
                        xsl = xmf_sb[:, ci * TSM:(ci + 1) * TSM]
                        tv = u_v[ci % 2]
                        b.op("vector", "tensor_mul", tv, xsl, rbcL[:])
                        b.op("vector", "tensor_sub",
                             xm_sb[:, c_g * TSM:(c_g + 1) * TSM], tv, mbcL[:],
                             inc=("V", 1))
                    yield
                xm_ready = b.cnt["V"]
                xm_free[0] = ("V", xm_ready)
                bc_ready = None
                for g in range(2):
                    if g == 1:
                        mx_load_group(h, s_i, 1)
                        b.dwait("vector", "XM")
                        b.dwait("scalar", "XM")
                        b.dwait("gpsimd", "XM")
                    b.wait("scalar", "V", ss["vh"].get(ss["pos"] - 1))
                    ets = list(range(g * CG, g * CG + CG)) + ([DX_T] if g == 0 else [])
                    dtm = {}
                    for et in ets:
                        buf = et % 2
                        xxs = "XX0" if buf == 0 else "XX1"
                        if et < DX_T:
                            xv = xxw_sb[:, buf * DC * P:(buf + 1) * DC * P]
                            src = sxx_d[et]
                            lhs_w, m_w = xv, P
                        else:
                            xv = xxw_sb[:, buf * DC * P: buf * DC * P + DC * 32]
                            src = sxbc_d[:]
                            lhs_w, m_w = xv, 32
                        s, v = xxw_free[buf]
                        b.wait("sync", s, v)
                        b.op("sync", "dma_start", out=xv, in_=src, inc=(xxs, 16))
                        pxk = 4 + (et % 2)
                        pairs = [(lhs_w[:, c_i * m_w:(c_i + 1) * m_w],
                                  xm_sb[:, c_i * TSM:(c_i + 1) * TSM])
                                 for c_i in range(DC)]
                        b.dwait("tensor", xxs)
                        tx = mm_group(pxk, pairs,
                                      out_ap=(ps[pxk][:, 0:TSM] if et < DX_T
                                              else ps[pxk][0:32, 0:TSM]),
                                      wait_rhs=(("V", xm_ready)
                                                if (g == 0 and et == 0) else None))
                        xxw_free[buf] = ("T", tx)
                        if et < DX_T:
                            ci = et - g * CG
                            pv = ps[pxk][:, 0:TSM]
                            na = ab_v[et % 2]
                            b.wait("scalar", "T", tx)
                            if ci >= 2:
                                b.wait("scalar", "V", dtm[ci - 2])
                            b.op("scalar", "activation", out=na, in_=pv, func=AF.Abs)
                            b.op("scalar", "activation", out=na, in_=na, func=AF.Exp,
                                 scale=-1.0)
                            b.op("scalar", "activation", out=na, in_=na, func=AF.Ln,
                                 bias=1.0, inc=("S", 1))
                            b.wait("vector", "S", b.cnt["S"])
                            b.wait("vector", "T", tx)
                            b.wait("vector", "S", dt_guard[0])
                            dsl = dt_sb[:, ci * TSM:(ci + 1) * TSM]
                            vi = b.op("vector", "scalar_tensor_tensor", out=dsl,
                                      in0=pv, scalar=0.0, in1=na, op0=OP.max,
                                      op1=OP.add, inc=("V", 1))
                            bank_free[pxk] = ("V", vi)
                            dtm[ci] = vi
                        else:
                            b.wait("vector", "T", tx)
                            b.dwait("vector", "BC")
                            bc_ready = b.op("vector", "tensor_copy", out=bc_sb[:],
                                            in_=ps[pxk][0:32, 0:TSM], inc=("V", 1))
                            bank_free[pxk] = ("V", bc_ready)
                        yield
                    ln_guard_T[0] = max(ln_guard_T[0], b.cnt["T"])
                    for ci in range(CG):
                        xsl = xmf_sb[:, ci * TSM:(ci + 1) * TSM]
                        dsl = dtx_sb[:, ci * TSM:(ci + 1) * TSM]
                        b.op("vector", "tensor_mul", dsl, xsl, rbcL[:])
                        b.op("vector", "tensor_sub", dsl, dsl, mbcL[:], inc=("V", 1))
                        b.op("vector", "scalar_tensor_tensor", out=xsl, in0=dsl,
                             scalar=dD_sb[:, g * CG + ci:g * CG + ci + 1], in1=xsl,
                             op0=OP.mult, op1=OP.add, inc=("V", 1))
                        b.op("vector", "tensor_mul", dsl, dsl,
                             dt_sb[:, ci * TSM:(ci + 1) * TSM], inc=("V", 1))
                        yield
                    dtx_done = b.cnt["V"]
                    bc_g_last = {}
                    vbc_by_n = {}

                    def bcast_emit(n):
                        nb2 = n % 2
                        slot = n % 4
                        bcrv = bcr_sb[0:1, 0:2 * TSM]
                        s, v = bcr_free[0]
                        b.wait("sync", s, v)
                        b.wait("sync", "V", bc_ready)
                        b.op("sync", "dma_start",
                             out=bcrv.rearrange("a (k t) -> a k t", k=2),
                             in_=bc_sb[n:n + 17:16, :], inc=("BC", 16))
                        kb, kc = (4, 5) if nb2 == 0 else (6, 7)
                        for k2 in (kb, kc):
                            s, v = bank_free[k2]
                            b.wait("tensor", s, v)
                        b.dwait("tensor", "BC")
                        t_b = mm_group(kb, [(ones32[0:1, :], bcrv[0:1, 0:TSM])],
                                       out_ap=ps[kb][:, 0:TSM])
                        t_c = mm_group(kc, [(ones32[0:1, :], bcrv[0:1, TSM:2 * TSM])],
                                       out_ap=ps[kc][:, 0:TSM])
                        bcr_free[0] = ("T", t_c)
                        bsl = bbc_sb[:, slot * TSM:(slot + 1) * TSM]
                        csl = cbc_sb[:, slot * TSM:
                                     slot * TSM + TSM // 2].bitcast(BF16)
                        b.wait("vector", "T", t_c)
                        b.wait("vector", "G", bc_g_last.get(n - 4))
                        b.op("vector", "tensor_copy", out=bsl, in_=ps[kb][:, 0:TSM])
                        vbc = b.op("vector", "tensor_copy", out=csl,
                                   in_=ps[kc][:, 0:TSM], inc=("V", 1))
                        bank_free[kb] = ("V", vbc)
                        bank_free[kc] = ("V", vbc)
                        vbc_by_n[n] = vbc

                    def emit_st():
                        p, idx2 = ss["pend_st"].pop(0)
                        b.wait("scalar", "V", ss["vh"][p])
                        s_st = b.op("scalar", "copy", out=st_sb[:, idx2:idx2 + 1],
                                    in_=h_v[p % 4][:, TSM - 1:TSM], inc=("S", 1))
                        ss["hs"][p] = s_st

                    def emit_add():
                        p, cg2, xs2, ws2 = ss["pend_add"].pop(0)
                        if cg2 % 4 == 3:
                            a_i = b.op("vector", "tensor_add", xs2, xs2, ws2,
                                       inc=("V", 1))
                            ss["w"][p] = ("V", a_i)
                        else:
                            b.wait("gpsimd", "V", ss["vw"][p])
                            a_i = b.op("gpsimd", "tensor_add", xs2, xs2, ws2,
                                       inc=("G", 1))
                            ss["w"][p] = ("G", a_i)

                    for n in range(NST):
                        if n == 0:
                            bcast_emit(0)
                            bcast_emit(1)
                        vbc = vbc_by_n[n]
                        for ci in range(CG):
                            c_g = g * CG + ci
                            pos = ss["pos"]
                            r4 = pos % 4
                            absl, usl, hsl, wsl = ab_v[r4], u_v[r4], h_v[r4], w_vb[r4]
                            dslt = dt_sb[:, ci * TSM:(ci + 1) * TSM]
                            dslx = dtx_sb[:, ci * TSM:(ci + 1) * TSM]
                            xsl = xmf_sb[:, ci * TSM:(ci + 1) * TSM]
                            idx = n * DC + c_g
                            if pos >= 4:
                                b.wait("scalar", "V", ss["vh"].get(pos - 4))
                            if n == 0 and ci == 0:
                                b.wait("scalar", "V", dtx_done)
                            s_ab = b.op("scalar", "activation", out=absl, in_=dslt,
                                        func=AF.Exp,
                                        scale=a_sb[:, c_g * NST + n:c_g * NST + n + 1],
                                        inc=("S", 1))
                            if pos >= 4:
                                b.wait("gpsimd", "V", ss["vh"].get(pos - 4))
                            if ci == 0:
                                b.wait("gpsimd", "V", vbc)
                            g_u = b.op("gpsimd", "tensor_mul", usl, dslx,
                                       bbc_sb[:, (n % 4) * TSM:(n % 4 + 1) * TSM],
                                       inc=("G", 1))
                            b.wait("vector", "S", s_ab)
                            b.wait("vector", "G", g_u)
                            if pos >= 4:
                                b.wait("vector", "S", ss["hs"][pos - 4])
                                pe, pv2 = ss["w"][pos - 4]
                                if pe == "G":
                                    b.wait("vector", "G", pv2)
                            v_h = b.op("vector", "tensor_tensor_scan", out=hsl,
                                       data0=absl, data1=usl,
                                       initial=st_sb[:, idx:idx + 1],
                                       op0=OP.mult, op1=OP.add, inc=("V", 1))
                            v_w = b.op("vector", "tensor_mul", wsl, hsl,
                                       cbc_sb[:, (n % 4) * TSM:(n % 4) * TSM
                                              + TSM // 2].bitcast(BF16),
                                       inc=("V", 1))
                            ss["vh"][pos] = v_h
                            ss["vw"][pos] = v_w
                            ss["ab"][pos] = v_h
                            ss["u"][pos] = v_h
                            ss["pend_st"].append((pos, idx))
                            ss["pend_add"].append((pos, c_g, xsl, wsl))
                            if len(ss["pend_st"]) > 3:
                                emit_st()
                            if len(ss["pend_add"]) > 2:
                                emit_add()
                            ss["pos"] = pos + 1
                            yield
                        bc_g_last[n] = b.cnt["G"]
                        if n + 2 < NST:
                            bcast_emit(n + 2)
                    while ss["pend_st"]:
                        emit_st()
                    while ss["pend_add"]:
                        emit_add()
                    dt_guard[0] = b.cnt["S"]
                    fv = b.cnt["V"]
                    fg = b.cnt["G"]
                    b.wait("gpsimd", "V", fv)
                    b.op("gpsimd", "dma_start",
                         out=dram3(g * CG * P, CG * P, t0g, TSM),
                         in_=re3(xmf_sb[:], CG), inc=("XW", 16))
                    xm_free[0] = ("V", fv)
                    yield

        def run_weave(main_gen, aux_gen, ratio):
            if ratio is None:
                for _ in main_gen:
                    pass
                for _ in aux_gen:
                    pass
                return
            acc = 0.0
            main_alive = aux_alive = True
            while main_alive:
                try:
                    next(main_gen)
                except StopIteration:
                    main_alive = False
                acc += ratio
                while acc >= 1.0 and aux_alive:
                    try:
                        next(aux_gen)
                    except StopIteration:
                        aux_alive = False
                    acc -= 1.0
            while aux_alive:
                try:
                    next(aux_gen)
                except StopIteration:
                    aux_alive = False

        def drain(gen):
            for _ in gen:
                pass

        def spill_x(h, after_v):
            b.wait("scalar", "V", after_v)
            for q in range(4):
                b.op("scalar", "dma_start",
                     out=xs_d[q * 4 * P:(q + 1) * 4 * P,
                              h * TH:(h + 1) * TH].rearrange("(c p) t -> p c t", p=P),
                     in_=re3(x_sb[:, q * 4 * TH:(q + 1) * 4 * TH], 4),
                     inc=(f"XS{q}", 16))
            if debug:
                b.op("scalar", "dma_start",
                     out=dbg_d[:, h * TH:(h + 1) * TH].rearrange("(c p) t -> p c t", p=P),
                     in_=re3(x_sb[:], DC), inc=("XS", 16))

        def load_x_from(mk):
            b.dwait("scalar", "XW")
            for q in range(4):
                b.dwait("scalar", f"XS{q}")
                b.dwait("scalar", f"XO{q}")
                b.op("scalar", "dma_start",
                     out=re3(x_sb[:, q * 4 * TH:(q + 1) * 4 * TH], 4), in_=mk(q),
                     inc=(f"XL{q}", 16))
            b.dwait("vector", "XL3")
            for q in range(3):
                b.dwait("vector", f"XL{q}")

        def store_out(h, after_v):
            b.wait("scalar", "V", after_v)
            for q in range(4):
                b.op("scalar", "dma_start",
                     out=out_d[q * 4 * P:(q + 1) * 4 * P,
                               h * TH:(h + 1) * TH].rearrange("(c p) t -> p c t", p=P),
                     in_=re3(x_sb[:, q * 4 * TH:(q + 1) * 4 * TH], 4),
                     inc=(f"XO{q}", 16))

        for eng in ("vector", "scalar", "tensor"):
            b.dwait(eng, "XL")
        v0 = 0
        for s_i in range(NSUB):
            v0 = ln_sub(s_i)
        q2_src[0] = q2_gen()
        for _ in range(70):
            next(q2_src[0])

        def q2_aux():
            while q2_src[0] is not None:
                try:
                    next(q2_src[0])
                except StopIteration:
                    q2_src[0] = None
                yield

        run_weave(ffn_gen(v0, jit=True), q2_aux(), 2.0)
        if debug:
            b.wait("scalar", "V", b.cnt["V"])
            b.op("scalar", "dma_start", out=dbg2_d, in_=sc_sb[:], inc=("XS", 16))
            b.op("scalar", "dma_start", out=dbg3_d, in_=srow[:], inc=("XS", 16))
        spill_x(0, b.cnt["V"])
        load_x_from(lambda q: x_d[q * 4 * P:(q + 1) * 4 * P,
                                  TH:2 * TH].rearrange("(c p) t -> p c t", p=P))
        for s_i in range(NSUB):
            v0 = ln_sub(s_i)
        run_weave(ffn_gen(v0), mixer_gen(0), WEAVE_RATIO)
        spill_x(1, b.cnt["V"])
        load_x_from(lambda q: dram3(q * 4 * P, 4 * P, 0, TH))
        for s_i in range(NSUB):
            v0 = ln_sub(s_i)
        run_weave(ffn_gen(v0), mixer_gen(1), WEAVE_RATIO)
        store_out(0, b.cnt["V"])
        load_x_from(lambda q: dram3(q * 4 * P, 4 * P, TH, TH))
        for s_i in range(NSUB):
            v0 = ln_sub(s_i)
        drain(ffn_gen(v0))
        store_out(1, b.cnt["V"])
        for q in range(4):
            b.dwait("scalar", f"XO{q}")
        if debug:
            b.dwait("scalar", "XS")

        blk_ctx = es.enter_context(nc.Block())

        def replay(name):
            def f(q):
                for meth, args, kwargs, inc in b.ev[name]:
                    ins = getattr(q, meth)(*args, **kwargs)
                    if inc is not None:
                        ins.then_inc(*inc)
            return f

        blk_ctx.sync(replay("sync"))
        blk_ctx.gpsimd(replay("gpsimd"))
        blk_ctx.scalar(replay("scalar"))
        blk_ctx.vector(replay("vector"))
        blk_ctx.tensor(replay("tensor"))
    return nc


def kernel(L0_attn_x, L0_attn_A_log, L0_attn_D, L0_attn_W_xproj,
           L0_ffn_W_up, L0_ffn_W_gate, L0_ffn_W_down, stage="full"):
    xs, wmaps = host_layout(L0_attn_x, L0_attn_A_log, L0_attn_D, L0_attn_W_xproj,
                            L0_ffn_W_up, L0_ffn_W_gate, L0_ffn_W_down)
    nc = build_nc(stage)
    in_maps = [dict(x_fm=xs[b_], **wmaps) for b_ in range(8)]
    res = run_bass_kernel_spmd(nc, in_maps, core_ids=list(range(8)))
    out = np.stack([np.asarray(r["out_fm"]).reshape(D, T).T for r in res.results])
    return np.ascontiguousarray(out.astype(np.float32))


# version 17
# speedup vs baseline: 1.0423x; 1.0423x over previous
import numpy as np
from contextlib import ExitStack

import concourse.bass as bass
import concourse.mybir as mybir
from concourse.bass_utils import run_bass_kernel_spmd

F32 = mybir.dt.float32
BF16 = mybir.dt.bfloat16
FP8 = mybir.dt.float8e4
AF = mybir.ActivationFunctionType
OP = mybir.AluOpType
AX = mybir.AxisListType.X

P = 128
D = 2048
DC = 16
T = 2048
TH = 1024
NH = 2
E = 8192
NET = 64
EBT = 16
NBLK = NET // EBT
NST = 16
TCF = 512
TSM = 256
NSUB = TH // TSM
DX_T = 16
CG = 8
WEAVE_RATIO = 5.0

ENGINES = ("sync", "gpsimd", "scalar", "vector", "tensor")
DMA_SEMS = ("WU0", "WU1", "WG0", "WG1", "WD0", "WD1", "XX0", "XX1",
            "WF0", "WF1", "QS0", "QS1", "QS2", "QS3", "XC0", "XC1", "SC0", "SC1",
            "XL", "XS", "XM", "XW", "XO", "BC", "AD", "XL1")
CMP_SEMS = ("T", "V", "S", "G")


class Prog:
    def __init__(self, sems):
        self.ev = {e: [] for e in ENGINES}
        self.sems = sems
        self.cnt = {k: 0 for k in sems}

    def op(self, eng, meth, *args, inc=None, **kwargs):
        tgt = None
        h = None
        if inc is not None:
            sem, n = inc
            self.cnt[sem] += n
            tgt = self.cnt[sem]
            h = (self.sems[sem], n)
        self.ev[eng].append((meth, args, kwargs, h))
        return tgt

    def wait(self, eng, sem, val):
        if val is None or val <= 0:
            return
        assert val <= self.cnt[sem], f"wait {sem}>={val} but only {self.cnt[sem]}"
        if sem in DMA_SEMS:
            assert val == self.cnt[sem], \
                f"DMA sem {sem} wait {val} != full {self.cnt[sem]}"
        self.ev[eng].append(("wait_ge", (self.sems[sem], val), {}, None))

    def dwait(self, eng, sem):
        self.wait(eng, sem, self.cnt[sem])


def host_layout(x, A_log, Dv, W_xproj, W_up, W_gate, W_down):
    c = np.ascontiguousarray
    xs = [c(np.asarray(x[b]).T) for b in range(8)]
    WuT = np.asarray(W_up).T
    WgT = np.asarray(W_gate).T
    WdT = np.asarray(W_down).T
    WxT = np.asarray(W_xproj).T
    xu = c(WuT.reshape(DC, P, NET, P).transpose(2, 1, 0, 3)).reshape(NET, P, DC * P)
    xg = c(WgT.reshape(DC, P, NET, P).transpose(2, 1, 0, 3)).reshape(NET, P, DC * P)
    xd = c(WdT.reshape(NET, P, DC, P).transpose(2, 1, 0, 3)).reshape(DC, P, NET * P)
    xx = c(WxT[:, :D].reshape(DC, P, DX_T, P).transpose(2, 1, 0, 3)).reshape(DX_T, P, DC * P)
    xbc = c(WxT[:, D:].reshape(DC, P, 32).transpose(1, 0, 2)).reshape(P, DC * 32)
    a_re = c(np.asarray(A_log).reshape(DC, P, NST).transpose(1, 0, 2)).reshape(P, DC * NST)
    d_re = c(np.asarray(Dv).reshape(DC, P).T)
    return xs, dict(xu=xu, xg=xg, xd=xd, xx=xx, xbc=xbc, a_re=a_re, d_re=d_re)


def build_nc(stage="full", debug=False):
    nc = bass.Bass("TRN2", target_bir_lowering=False)

    x_d = nc.dram_tensor("x_fm", [D, T], F32, kind="ExternalInput").ap()
    xu_d = nc.dram_tensor("xu", [NET, P, DC * P], F32, kind="ExternalInput").ap()
    xg_d = nc.dram_tensor("xg", [NET, P, DC * P], F32, kind="ExternalInput").ap()
    xd_d = nc.dram_tensor("xd", [DC, P, NET * P], F32, kind="ExternalInput").ap()
    xx_d = nc.dram_tensor("xx", [DX_T, P, DC * P], F32, kind="ExternalInput").ap()
    xbc_d = nc.dram_tensor("xbc", [P, DC * 32], F32, kind="ExternalInput").ap()
    a_d = nc.dram_tensor("a_re", [P, DC * NST], F32, kind="ExternalInput").ap()
    dD_d = nc.dram_tensor("d_re", [P, DC], F32, kind="ExternalInput").ap()
    out_d = nc.dram_tensor("out_fm", [D, T], F32, kind="ExternalOutput").ap()
    squ_d = nc.dram_tensor("squ", [NET, P, DC * P], FP8, kind="Internal").ap()
    sqg_d = nc.dram_tensor("sqg", [NET, P, DC * P], FP8, kind="Internal").ap()
    sqd_d = nc.dram_tensor("sqd", [DC, P, NET * P], FP8, kind="Internal").ap()
    sxx_d = nc.dram_tensor("sxx", [DX_T, P, DC * P], BF16, kind="Internal").ap()
    sxbc_d = nc.dram_tensor("sxbc", [P, DC * 32], BF16, kind="Internal").ap()
    xs_d = nc.dram_tensor("xs", [D, T], F32,
                          kind="ExternalOutput" if debug else "Internal").ap()
    dbg_d = (nc.dram_tensor("dbg", [D, T], F32, kind="ExternalOutput").ap()
             if debug else None)
    dbg2_d = (nc.dram_tensor("dbg2", [P, 16], F32, kind="ExternalOutput").ap()
              if debug else None)
    dbg3_d = (nc.dram_tensor("dbg3", [1, 16], F32, kind="ExternalOutput").ap()
              if debug else None)
    if debug:
        squ_d = nc.dram_tensor("squ2", [NET, P, DC * P], FP8, kind="ExternalOutput").ap()

    es = ExitStack()
    sb = lambda name, shape, dt: es.enter_context(nc.sbuf_tensor(name, shape, dt))
    with es:
        x_sb = sb("x_sb", [P, DC * TH], F32)
        xln_sb = sb("xln_sb", [P, DC * TH], BF16)
        far = sb("far", [P, 4608], F32)
        hid = far[:, 0:4096].bitcast(BF16)
        silu_v = far[:, 4096:4608].bitcast(BF16)
        dt_sb = sb("dt_sb", [P, CG * TSM], F32)
        dtx_sb = sb("dtx_sb", [P, CG * TSM], F32)
        xm_sb = sb("xm_sb", [P, DC * TSM], BF16)
        xmf_sb = sb("xmf_sb", [P, CG * TSM], F32)
        scn_sb = sb("scn_sb", [P, 4096], F32)
        ab_v = [scn_sb[:, k * TSM:(k + 1) * TSM] for k in range(4)]
        u_v = [scn_sb[:, (4 + k) * TSM:(5 + k) * TSM] for k in range(4)]
        h_v = [scn_sb[:, (8 + k) * TSM:(8 + k) * TSM + TSM // 2].bitcast(BF16)
               for k in range(4)]
        w_vb = [scn_sb[:, (12 + k) * TSM:(12 + k) * TSM + TSM // 2].bitcast(BF16)
                for k in range(4)]
        w_v = [scn_sb[:, (12 + k) * TSM:(13 + k) * TSM] for k in range(4)]
        q_wf = [dt_sb[:, 0:2048], dtx_sb[:, 0:2048]]
        q_qa = [xm_sb[:, 0:2048], xm_sb[:, 2048:4096]]
        q_q8 = [xmf_sb[:, 0:512].bitcast(FP8), xmf_sb[:, 512:1024].bitcast(FP8)]
        red_v = scn_sb[:, 0:200]
        wu_sb = sb("wu_sb", [P, 2 * DC * P], FP8)
        wg_sb = sb("wg_sb", [P, 2 * DC * P], FP8)
        wd_sb = sb("wd_sb", [P, 2 * EBT * P], FP8)
        xxw_sb = sb("xxw_sb", [P, 2 * DC * P], BF16)
        bbc_sb = sb("bbc_sb", [P, 4 * TSM], F32)
        cbc_sb = sb("cbc_sb", [P, 4 * TSM], F32)
        bc_sb = sb("bc_sb", [32, TSM], F32)
        bcr_sb = sb("bcr_sb", [1, 2 * TSM], F32)
        st_sb = sb("st_sb", [P, 256], F32)
        a_sb = sb("a_sb", [P, DC * NST], F32)
        dD_sb = sb("dD_sb", [P, DC], F32)
        gev_sb = sb("gev_sb", [P, 2 * TCF], BF16)
        sqL = sb("sqL", [P, 4 * TSM], BF16)
        rowsL = sb("rowsL", [1, 2 * TSM], F32)
        mbcL = sb("mbcL", [P, TSM], F32)
        rbcL = sb("rbcL", [P, TSM], F32)
        tmpL = sb("tmpL", [P, 2 * TSM], F32)
        ones32 = sb("ones32", [32, P], F32)
        onesf = sb("onesf", [P, 1], F32)
        onesb = sb("onesb", [P, 1], BF16)
        cst = sb("cst", [P, 3], F32)
        sc_sb = sb("sc_sb", [P, 16], F32)
        srow = sb("srow", [1, 16], F32)
        nc.const_aps.register(F32, 1.0, cst[:, 0:1])
        nc.const_aps.register(F32, 1e-5, cst[:, 1:2])
        nc.const_aps.register(F32, -1.0, cst[:, 2:3])

        ps = [es.enter_context(nc.psum_tensor(f"ps{i}", [P, 512], F32)) for i in range(8)]
        semh = {k: es.enter_context(nc.semaphore(k)) for k in DMA_SEMS + CMP_SEMS}
        b = Prog(semh)

        SU, SG, GD, THU, THG, THD, THUN, THGN, THDN, GDS = range(10)
        sc = lambda j: sc_sb[:, j:j + 1]
        bank_free = [("T", 0)] * 8

        def mm_group(bank, pairs, out_ap=None, wait_rhs=None):
            s, v = bank_free[bank]
            b.wait("tensor", s, v)
            if wait_rhs is not None:
                b.wait("tensor", wait_rhs[0], wait_rhs[1])
            o = out_ap if out_ap is not None else ps[bank][:]
            n = len(pairs)
            for i, (l, r) in enumerate(pairs):
                last = i == n - 1
                b.op("tensor", "matmul", o, l, r, start=(i == 0), stop=last,
                     inc=(("T", 1) if last else None))
            return b.cnt["T"]

        def re3(ap_sb, ngrp):
            return ap_sb.rearrange("p (c t) -> p c t", c=ngrp)

        def dram3(rows0, nrows, c0, w):
            return xs_d[rows0:rows0 + nrows, c0:c0 + w].rearrange(
                "(c p) t -> p c t", p=P)

        b.op("vector", "memset", cst[:, 0:1], 1.0)
        b.op("vector", "memset", cst[:, 1:2], 1e-5)
        b.op("vector", "memset", cst[:, 2:3], -1.0)
        b.op("vector", "memset", ones32[:], 1.0)
        b.op("vector", "memset", onesf[:], 1.0)
        b.op("vector", "memset", onesb[:], 1.0)
        b.op("vector", "memset", st_sb[:], 0.0, inc=("V", 1))
        b.op("sync", "dma_start", out=a_sb[:], in_=a_d[:], inc=("AD", 16))
        b.op("sync", "dma_start", out=dD_sb[:], in_=dD_d[:], inc=("AD", 16))
        b.dwait("scalar", "AD")
        b.op("scalar", "activation", out=a_sb[:], in_=a_sb[:], func=AF.Exp, inc=("S", 1))
        b.wait("vector", "S", b.cnt["S"])
        b.op("vector", "tensor_scalar_mul", a_sb[:], a_sb[:], -1.0, inc=("V", 1))

        b.op("scalar", "dma_start", out=re3(x_sb[:], DC),
             in_=x_d[:, 0:TH].rearrange("(c p) t -> p c t", p=P), inc=("XL", 16))
        for et in range(DX_T):
            buf = et % 2
            scs = "SC0" if buf == 0 else "SC1"
            xcs = "XC0" if buf == 0 else "XC1"
            xv = xxw_sb[:, buf * DC * P:(buf + 1) * DC * P]
            b.dwait("gpsimd", scs)
            b.op("gpsimd", "dma_start", out=xv, in_=xx_d[et], inc=(xcs, 16))
            b.dwait("scalar", xcs)
            b.op("scalar", "dma_start", out=sxx_d[et], in_=xv, inc=(scs, 16))
        b.dwait("gpsimd", "SC0")
        xbcv = xxw_sb[:, 0:DC * 32]
        b.op("gpsimd", "dma_start", out=xbcv, in_=xbc_d[:], inc=("XC0", 16))
        b.dwait("scalar", "XC0")
        b.op("scalar", "dma_start", out=sxbc_d[:], in_=xbcv, inc=("SC0", 16))
        xxw_free = [("SC0", None), ("SC1", None)]

        wf_free = [("V", 0), ("V", 0)]

        def scale_tile(i, src_ap, queue):
            buf = i % 2
            wfs = "WF0" if buf == 0 else "WF1"
            s, v = wf_free[buf]
            b.wait(queue, s, v)
            b.op(queue, "dma_start", out=q_wf[buf], in_=src_ap, inc=(wfs, 16))
            b.dwait("vector", wfs)
            ci = b.op("vector", "reduce_sum", out=red_v[:, i:i + 1], in_=q_wf[buf],
                      axis=AX, apply_absolute_value=True, inc=("V", 1))
            wf_free[buf] = ("V", ci)

        def finalize_scale(m):
            b.op("vector", "reduce_sum", out=red_v[:, 192 + m:193 + m],
                 in_=red_v[:, m * 64:(m + 1) * 64], axis=AX, inc=("V", 1))
            vfin = b.cnt["V"]
            tcnt = mm_group(6, [(onesf[:], red_v[:, 192 + m:193 + m])],
                            out_ap=ps[6][0:1, 0:1], wait_rhs=("V", vfin))
            b.wait("vector", "T", tcnt)
            c0 = 1.0 / 16777216.0
            mulmap = {0: -1.0, 1: -1.0, 2: -0.5}
            b.op("vector", "tensor_scalar", out=srow[0:1, m:m + 1],
                 in0=ps[6][0:1, 0:1], scalar1=c0, scalar2=1e-8,
                 op0=OP.mult, op1=OP.add)
            b.op("vector", "tensor_scalar", out=srow[0:1, 3 + m:4 + m],
                 in0=ps[6][0:1, 0:1], scalar1=mulmap[m] * c0,
                 scalar2=mulmap[m] * 1e-8, op0=OP.mult, op1=OP.add)
            b.op("vector", "tensor_scalar", out=srow[0:1, 6 + m:7 + m],
                 in0=ps[6][0:1, 0:1], scalar1=0.5 * c0, scalar2=0.5e-8,
                 op0=OP.mult, op1=OP.add)
            b.op("vector", "tensor_scalar", out=srow[0:1, 9 + m:10 + m],
                 in0=ps[6][0:1, 0:1], scalar1=-0.5 * c0, scalar2=-0.5e-8,
                 op0=OP.mult, op1=OP.add, inc=("V", 1))
            bank_free[6] = ("V", b.cnt["V"])
            vscal = b.cnt["V"]
            for j, sr in [((SU, 3), (THU, 6), (THUN, 9)),
                          ((SG, 4), (THG, 7), (THGN, 10)),
                          ((GD, 5), (THD, 8), (THDN, 11))][m]:
                tcnt = mm_group(6, [(ones32[0:1, :], srow[0:1, sr:sr + 1])],
                                out_ap=ps[6][:, 0:1], wait_rhs=("V", vscal))
                b.wait("vector", "T", tcnt)
                b.op("vector", "tensor_copy", out=sc(j), in_=ps[6][:, 0:1],
                     inc=("V", 1))
                if j == GD:
                    b.op("vector", "tensor_mul", sc(GDS), ps[6][:, 0:1], sc(SG),
                         inc=("V", 1))
                bank_free[6] = ("V", b.cnt["V"])

        for et in range(NET):
            scale_tile(et, xu_d[et], "sync")
        for et in range(NET):
            scale_tile(NET + et, xg_d[et], "sync")
        finalize_scale(0)
        finalize_scale(1)

        q2 = []
        for r in range(32):
            q2 += [(xu_d[2 * r], squ_d[2 * r], 0, ("u", 2 * r)),
                   (xu_d[2 * r + 1], squ_d[2 * r + 1], 0, ("u", 2 * r + 1)),
                   (xg_d[2 * r], sqg_d[2 * r], 1, ("g", 2 * r)),
                   (xg_d[2 * r + 1], sqg_d[2 * r + 1], 1, ("g", 2 * r + 1))]
            s4, dt0 = r // 8, 2 * (r % 8)
            for dt_i in (dt0, dt0 + 1):
                sl = slice(s4 * 2048, (s4 + 1) * 2048)
                q2.append((xd_d[dt_i][:, sl], sqd_d[dt_i][:, sl], 2,
                           ("d", dt_i, s4)))
        qmark = {}
        qa_free = [("V", 0), ("V", 0)]
        qs_names = ("QS0", "QS1", "QS2", "QS3")
        q2_src = [None]

        def need_q(key):
            while key not in qmark and q2_src[0] is not None:
                try:
                    next(q2_src[0])
                except StopIteration:
                    q2_src[0] = None
            b.wait("sync", "G", qmark[key])

        def q2_gen():
            for k in range(64):
                dt_i, s4 = k // 4, k % 4
                scale_tile(128 + k,
                           xd_d[dt_i][:, s4 * 2048:(s4 + 1) * 2048], "gpsimd")
                yield
            finalize_scale(2)
            pend = []
            for i, (src_ap, dst, mat, key) in enumerate(q2):
                buf = i % 2
                wfs = "WF0" if buf == 0 else "WF1"
                qss = qs_names[i % 4]
                s, v = wf_free[buf]
                b.wait("gpsimd", s, v)
                b.op("gpsimd", "dma_start", out=q_wf[buf], in_=src_ap, inc=(wfs, 16))
                b.dwait("vector", wfs)
                s, v = qa_free[buf]
                b.wait("vector", s, v)
                b.op("vector", "tensor_single_scalar", out=q_qa[buf], in_=q_wf[buf],
                     scalar=sc(THU + mat), op=OP.is_gt)
                b.dwait("vector", qss)
                vi = b.op("vector", "scalar_tensor_tensor", out=q_q8[buf],
                          in0=q_wf[buf], scalar=sc(THUN + mat), in1=q_qa[buf],
                          op0=OP.is_lt, op1=OP.subtract, inc=("V", 1))
                wf_free[buf] = ("V", vi)
                qa_free[buf] = ("V", vi)
                b.wait("scalar", "V", vi)
                b.op("scalar", "dma_start", out=dst, in_=q_q8[buf], inc=(qss, 16))
                pend.append((key, qss))
                if len(pend) > 2:
                    k2, q2s = pend.pop(0)
                    b.dwait("gpsimd", q2s)
                    qmark[k2] = b.op("gpsimd", "engine_nop", inc=("G", 1))
                yield
            while pend:
                k2, q2s = pend.pop(0)
                b.dwait("gpsimd", q2s)
                qmark[k2] = b.op("gpsimd", "engine_nop", inc=("G", 1))

        xxw_free = [("SC0", b.cnt["SC0"]), ("SC1", b.cnt["SC1"])]

        ln_guard_T = [0]
        def ln_sub(s_i, after_counts=None):
            t0 = s_i * TSM
            b.wait("scalar", "V", b.cnt["V"])
            b.wait("scalar", "T", ln_guard_T[0])
            for c_i in range(DC):
                if c_i == 0:
                    b.dwait("scalar", "XL")
                elif c_i == 8:
                    b.dwait("scalar", "XL1")
                b.op("scalar", "activation", out=xm_sb[:, c_i * TSM:(c_i + 1) * TSM],
                     in_=x_sb[:, c_i * TH + t0: c_i * TH + t0 + TSM], func=AF.Copy,
                     inc=(("S", 1) if c_i == DC - 1 else None))
            tmarks = []
            for c_i in range(DC):
                cb = c_i % 4
                xbv = xm_sb[:, c_i * TSM:(c_i + 1) * TSM]
                sqv = sqL[:, cb * TSM:(cb + 1) * TSM]
                if c_i >= 4:
                    b.wait("scalar", "T", tmarks[c_i - 4])
                s_c = b.op("scalar", "activation", out=sqv, in_=xbv, func=AF.Square,
                           inc=("S", 1))
                if c_i == 0:
                    for k in (0, 1):
                        s, v = bank_free[k]
                        b.wait("tensor", s, v)
                b.wait("tensor", "S", s_c)
                b.op("tensor", "matmul", ps[0][0:1, 0:TSM], onesb[:], xbv,
                     start=(c_i == 0), stop=(c_i == DC - 1))
                b.op("tensor", "matmul", ps[1][0:1, 0:TSM], onesb[:], sqv,
                     start=(c_i == 0), stop=(c_i == DC - 1), inc=("T", 1))
                tmarks.append(b.cnt["T"])
            tstat = b.cnt["T"]
            ln_guard_T[0] = tstat
            b.wait("vector", "T", tstat)
            b.op("vector", "tensor_scalar_mul", rowsL[0:1, 0:TSM],
                 ps[0][0:1, 0:TSM], 1.0 / D)
            b.op("vector", "tensor_scalar_mul", rowsL[0:1, TSM:2 * TSM],
                 ps[1][0:1, 0:TSM], 1.0 / D, inc=("V", 1))
            bank_free[0] = ("V", b.cnt["V"])
            bank_free[1] = ("V", b.cnt["V"])
            b.op("vector", "scalar_tensor_tensor", out=tmpL[0:1, 0:TSM],
                 in0=rowsL[0:1, 0:TSM], scalar=-1.0, in1=rowsL[0:1, 0:TSM],
                 op0=OP.mult, op1=OP.mult)
            b.op("vector", "tensor_add", rowsL[0:1, TSM:2 * TSM],
                 rowsL[0:1, TSM:2 * TSM], tmpL[0:1, 0:TSM], inc=("V", 1))
            b.wait("scalar", "V", b.cnt["V"])
            b.op("scalar", "activation", out=rowsL[0:1, TSM:2 * TSM],
                 in_=rowsL[0:1, TSM:2 * TSM], func=AF.Sqrt, bias=1e-5, inc=("S", 1))
            b.wait("vector", "S", b.cnt["S"])
            b.op("vector", "reciprocal", out=rowsL[0:1, TSM:2 * TSM],
                 in_=rowsL[0:1, TSM:2 * TSM])
            b.op("vector", "tensor_mul", rowsL[0:1, 0:TSM], rowsL[0:1, 0:TSM],
                 rowsL[0:1, TSM:2 * TSM], inc=("V", 1))
            vrows = b.cnt["V"]
            t_r = mm_group(2, [(ones32[0:1, :], rowsL[0:1, TSM:2 * TSM])],
                           out_ap=ps[2][:, 0:TSM], wait_rhs=("V", vrows))
            t_m = mm_group(3, [(ones32[0:1, :], rowsL[0:1, 0:TSM])],
                           out_ap=ps[3][:, 0:TSM])
            b.wait("vector", "T", t_m)
            b.op("vector", "tensor_copy", out=rbcL[:], in_=ps[2][:, 0:TSM])
            b.op("vector", "tensor_copy", out=mbcL[:], in_=ps[3][:, 0:TSM],
                 inc=("V", 1))
            bank_free[2] = ("V", b.cnt["V"])
            bank_free[3] = ("V", b.cnt["V"])
            for c_i in range(DC):
                xsl = x_sb[:, c_i * TH + t0: c_i * TH + t0 + TSM]
                tv = tmpL[:, (c_i % 2) * TSM:(c_i % 2 + 1) * TSM]
                b.op("vector", "tensor_mul", tv, xsl, rbcL[:])
                b.op("vector", "tensor_sub",
                     xln_sb[:, c_i * TH + t0: c_i * TH + t0 + TSM], tv, mbcL[:],
                     inc=("V", 1))
            return b.cnt["V"]

        wu_free = [("T", 0), ("T", 0)]
        wg_free = [("T", 0), ("T", 0)]
        wd_free = [("T", 0), ("T", 0)]
        hid_guard = [("T", 0)]
        silu_guard = [("V", 0), ("V", 0)]
        gev_guard = [("V", 0), ("V", 0)]

        def ffn_gen(xln_ready, jit=False):
            if not jit:
                b.wait("sync", "G", qmark.get(("d", DC - 1, NBLK - 1)))
            b.wait("tensor", "V", xln_ready)
            for tc in range(2):
                for blk in range(NBLK):
                    prev_guard = hid_guard[0]
                    for j in range(EBT):
                        et = blk * EBT + j
                        buf = j % 2
                        wus = "WU0" if buf == 0 else "WU1"
                        wgs = "WG0" if buf == 0 else "WG1"
                        wuv = wu_sb[:, buf * DC * P:(buf + 1) * DC * P]
                        wgv = wg_sb[:, buf * DC * P:(buf + 1) * DC * P]
                        s, v = wu_free[buf]
                        b.wait("sync", s, v)
                        if jit:
                            need_q(("u", et))
                        b.op("sync", "dma_start", out=wuv, in_=squ_d[et], inc=(wus, 16))
                        s, v = wg_free[buf]
                        b.wait("sync", s, v)
                        if jit:
                            need_q(("g", et))
                        b.op("sync", "dma_start", out=wgv, in_=sqg_d[et], inc=(wgs, 16))
                        pu, pg = buf, 2 + buf
                        xsl = [xln_sb[:, c_i * TH + tc * TCF: c_i * TH + (tc + 1) * TCF]
                               for c_i in range(DC)]
                        b.dwait("tensor", wus)
                        tu = mm_group(pu, [(wuv[:, c_i * P:(c_i + 1) * P], xsl[c_i])
                                           for c_i in range(DC)])
                        b.dwait("tensor", wgs)
                        tg = mm_group(pg, [(wgv[:, c_i * P:(c_i + 1) * P], xsl[c_i])
                                           for c_i in range(DC)])
                        wu_free[buf] = ("T", tu)
                        wg_free[buf] = ("T", tg)
                        b.wait("scalar", "T", tu)
                        s, v = silu_guard[buf]
                        b.wait("scalar", s, v)
                        s_i2 = b.op("scalar", "activation",
                                    out=silu_v[:, buf * TCF:(buf + 1) * TCF],
                                    in_=ps[pu][:], func=AF.Silu, scale=sc(SU),
                                    inc=("S", 1))
                        bank_free[pu] = ("S", s_i2)
                        b.wait("scalar", "T", tg)
                        s, v = gev_guard[buf]
                        b.wait("scalar", s, v)
                        s_g = b.op("scalar", "activation",
                                   out=gev_sb[:, buf * TCF:(buf + 1) * TCF],
                                   in_=ps[pg][:], func=AF.Copy, inc=("S", 1))
                        bank_free[pg] = ("S", s_g)
                        b.wait("vector", "S", s_g)
                        if j <= 1:
                            s, v = prev_guard
                            b.wait("vector", s, v)
                        gi = b.op("vector", "tensor_mul",
                                  hid[:, j * TCF:(j + 1) * TCF],
                                  gev_sb[:, buf * TCF:(buf + 1) * TCF],
                                  silu_v[:, buf * TCF:(buf + 1) * TCF],
                                  inc=("V", 1))
                        gev_guard[buf] = ("V", gi)
                        silu_guard[buf] = ("V", gi)
                        yield
                    hid_ready = b.cnt["V"]
                    for dt_i in range(DC):
                        buf = dt_i % 2
                        wds = "WD0" if buf == 0 else "WD1"
                        wdv = wd_sb[:, buf * EBT * P:(buf + 1) * EBT * P]
                        s, v = wd_free[buf]
                        b.wait("sync", s, v)
                        if jit:
                            need_q(("d", dt_i, blk))
                        b.op("sync", "dma_start", out=wdv,
                             in_=sqd_d[dt_i][:, blk * EBT * P:(blk + 1) * EBT * P],
                             inc=(wds, 16))
                        pd = buf
                        b.dwait("tensor", wds)
                        td = mm_group(pd, [(wdv[:, ce * P:(ce + 1) * P],
                                            hid[:, ce * TCF:(ce + 1) * TCF])
                                           for ce in range(EBT)],
                                      wait_rhs=(("V", hid_ready) if dt_i == 0 else None))
                        wd_free[buf] = ("T", td)
                        xout = x_sb[:, dt_i * TH + tc * TCF: dt_i * TH + (tc + 1) * TCF]
                        b.wait("vector", "T", td)
                        vi = b.op("vector", "scalar_tensor_tensor", out=xout,
                                  in0=ps[pd][:], scalar=sc(GDS), in1=xout,
                                  op0=OP.mult, op1=OP.add, inc=("V", 1))
                        bank_free[pd] = ("V", vi)
                        yield
                    hid_guard[0] = ("T", b.cnt["T"])

        scan_state = {"pos": 0, "ab": {}, "u": {}, "hs": {}, "w": {},
                      "vh": {}, "vw": {}, "pend_st": [], "pend_add": []}
        bcr_free = [("T", 0)]
        xm_free = [("V", 0)]
        dt_guard = [0]

        def mx_load_group(h, s_i, g):
            t0 = h * TH + s_i * TSM
            s, v = xm_free[0]
            b.wait("gpsimd", s, v)
            b.dwait("gpsimd", "XW")
            b.dwait("gpsimd", "XS")
            b.dwait("gpsimd", "QS0")
            b.dwait("gpsimd", "QS1")
            b.op("gpsimd", "dma_start", out=re3(xmf_sb[:], CG),
                 in_=dram3(g * CG * P, CG * P, t0, TSM), inc=("XM", 16))

        def mixer_gen(h):
            ss = scan_state
            for s_i in range(NSUB):
                t0g = h * TH + s_i * TSM
                ln_tm = []
                for g in range(2):
                    mx_load_group(h, s_i, g)
                    b.dwait("scalar", "XM")
                    if g == 0:
                        b.wait("scalar", "T", ln_guard_T[0])
                    for ci in range(CG):
                        c_g = g * CG + ci
                        b.op("scalar", "activation",
                             out=xm_sb[:, c_g * TSM:(c_g + 1) * TSM],
                             in_=xmf_sb[:, ci * TSM:(ci + 1) * TSM], func=AF.Copy,
                             inc=(("S", 1) if ci == CG - 1 else None))
                    for ci in range(CG):
                        c_g = g * CG + ci
                        cb = c_g % 4
                        xbv = xm_sb[:, c_g * TSM:(c_g + 1) * TSM]
                        sqv = sqL[:, cb * TSM:(cb + 1) * TSM]
                        if c_g >= 4:
                            b.wait("scalar", "T", ln_tm[c_g - 4])
                        s_c = b.op("scalar", "activation", out=sqv, in_=xbv,
                                   func=AF.Square, inc=("S", 1))
                        if c_g == 0:
                            for k in (6, 7):
                                s, v = bank_free[k]
                                b.wait("tensor", s, v)
                        b.wait("tensor", "S", s_c)
                        b.op("tensor", "matmul", ps[6][0:1, 0:TSM], onesb[:], xbv,
                             start=(c_g == 0), stop=(c_g == DC - 1))
                        b.op("tensor", "matmul", ps[7][0:1, 0:TSM], onesb[:], sqv,
                             start=(c_g == 0), stop=(c_g == DC - 1), inc=("T", 1))
                        ln_tm.append(b.cnt["T"])
                        yield
                    xm_free[0] = ("S", b.cnt["S"])
                tstat = b.cnt["T"]
                ln_guard_T[0] = tstat
                b.wait("vector", "T", tstat)
                b.op("vector", "tensor_scalar_mul", rowsL[0:1, 0:TSM],
                     ps[6][0:1, 0:TSM], 1.0 / D)
                b.op("vector", "tensor_scalar_mul", rowsL[0:1, TSM:2 * TSM],
                     ps[7][0:1, 0:TSM], 1.0 / D, inc=("V", 1))
                bank_free[6] = ("V", b.cnt["V"])
                bank_free[7] = ("V", b.cnt["V"])
                b.op("vector", "scalar_tensor_tensor", out=tmpL[0:1, 0:TSM],
                     in0=rowsL[0:1, 0:TSM], scalar=-1.0, in1=rowsL[0:1, 0:TSM],
                     op0=OP.mult, op1=OP.mult)
                b.op("vector", "tensor_add", rowsL[0:1, TSM:2 * TSM],
                     rowsL[0:1, TSM:2 * TSM], tmpL[0:1, 0:TSM], inc=("V", 1))
                b.wait("scalar", "V", b.cnt["V"])
                b.op("scalar", "activation", out=rowsL[0:1, TSM:2 * TSM],
                     in_=rowsL[0:1, TSM:2 * TSM], func=AF.Sqrt, bias=1e-5,
                     inc=("S", 1))
                b.wait("vector", "S", b.cnt["S"])
                b.op("vector", "reciprocal", out=rowsL[0:1, TSM:2 * TSM],
                     in_=rowsL[0:1, TSM:2 * TSM])
                b.op("vector", "tensor_mul", rowsL[0:1, 0:TSM], rowsL[0:1, 0:TSM],
                     rowsL[0:1, TSM:2 * TSM], inc=("V", 1))
                vrows = b.cnt["V"]
                t_r = mm_group(6, [(ones32[0:1, :], rowsL[0:1, TSM:2 * TSM])],
                               out_ap=ps[6][:, 0:TSM], wait_rhs=("V", vrows))
                t_m = mm_group(7, [(ones32[0:1, :], rowsL[0:1, 0:TSM])],
                               out_ap=ps[7][:, 0:TSM])
                b.wait("vector", "T", t_m)
                b.op("vector", "tensor_copy", out=rbcL[:], in_=ps[6][:, 0:TSM])
                b.op("vector", "tensor_copy", out=mbcL[:], in_=ps[7][:, 0:TSM],
                     inc=("V", 1))
                bank_free[6] = ("V", b.cnt["V"])
                bank_free[7] = ("V", b.cnt["V"])
                yield
                b.wait("vector", "G", b.cnt["G"])
                for rep in range(2):
                    g = 1 - rep
                    if rep == 1:
                        xm_free[0] = ("V", b.cnt["V"])
                        mx_load_group(h, s_i, 0)
                        b.dwait("vector", "XM")
                    for ci in range(CG):
                        c_g = g * CG + ci
                        xsl = xmf_sb[:, ci * TSM:(ci + 1) * TSM]
                        tv = u_v[ci % 2]
                        b.op("vector", "tensor_mul", tv, xsl, rbcL[:])
                        b.op("vector", "tensor_sub",
                             xm_sb[:, c_g * TSM:(c_g + 1) * TSM], tv, mbcL[:],
                             inc=("V", 1))
                    yield
                xm_ready = b.cnt["V"]
                xm_free[0] = ("V", xm_ready)
                bc_ready = None
                for g in range(2):
                    if g == 1:
                        mx_load_group(h, s_i, 1)
                        b.dwait("vector", "XM")
                        b.dwait("scalar", "XM")
                        b.dwait("gpsimd", "XM")
                    b.wait("scalar", "V", ss["vh"].get(ss["pos"] - 1))
                    ets = list(range(g * CG, g * CG + CG)) + ([DX_T] if g == 0 else [])
                    dtm = {}
                    for et in ets:
                        buf = et % 2
                        xxs = "XX0" if buf == 0 else "XX1"
                        if et < DX_T:
                            xv = xxw_sb[:, buf * DC * P:(buf + 1) * DC * P]
                            src = sxx_d[et]
                            lhs_w, m_w = xv, P
                        else:
                            xv = xxw_sb[:, buf * DC * P: buf * DC * P + DC * 32]
                            src = sxbc_d[:]
                            lhs_w, m_w = xv, 32
                        s, v = xxw_free[buf]
                        b.wait("sync", s, v)
                        b.op("sync", "dma_start", out=xv, in_=src, inc=(xxs, 16))
                        pxk = 4 + (et % 2)
                        pairs = [(lhs_w[:, c_i * m_w:(c_i + 1) * m_w],
                                  xm_sb[:, c_i * TSM:(c_i + 1) * TSM])
                                 for c_i in range(DC)]
                        b.dwait("tensor", xxs)
                        tx = mm_group(pxk, pairs,
                                      out_ap=(ps[pxk][:, 0:TSM] if et < DX_T
                                              else ps[pxk][0:32, 0:TSM]),
                                      wait_rhs=(("V", xm_ready)
                                                if (g == 0 and et == 0) else None))
                        xxw_free[buf] = ("T", tx)
                        if et < DX_T:
                            ci = et - g * CG
                            pv = ps[pxk][:, 0:TSM]
                            na = ab_v[et % 2]
                            b.wait("scalar", "T", tx)
                            if ci >= 2:
                                b.wait("scalar", "V", dtm[ci - 2])
                            b.op("scalar", "activation", out=na, in_=pv, func=AF.Abs)
                            b.op("scalar", "activation", out=na, in_=na, func=AF.Exp,
                                 scale=-1.0)
                            b.op("scalar", "activation", out=na, in_=na, func=AF.Ln,
                                 bias=1.0, inc=("S", 1))
                            b.wait("vector", "S", b.cnt["S"])
                            b.wait("vector", "T", tx)
                            b.wait("vector", "S", dt_guard[0])
                            dsl = dt_sb[:, ci * TSM:(ci + 1) * TSM]
                            vi = b.op("vector", "scalar_tensor_tensor", out=dsl,
                                      in0=pv, scalar=0.0, in1=na, op0=OP.max,
                                      op1=OP.add, inc=("V", 1))
                            bank_free[pxk] = ("V", vi)
                            dtm[ci] = vi
                        else:
                            b.wait("vector", "T", tx)
                            b.dwait("vector", "BC")
                            bc_ready = b.op("vector", "tensor_copy", out=bc_sb[:],
                                            in_=ps[pxk][0:32, 0:TSM], inc=("V", 1))
                            bank_free[pxk] = ("V", bc_ready)
                        yield
                    ln_guard_T[0] = max(ln_guard_T[0], b.cnt["T"])
                    for ci in range(CG):
                        xsl = xmf_sb[:, ci * TSM:(ci + 1) * TSM]
                        dsl = dtx_sb[:, ci * TSM:(ci + 1) * TSM]
                        b.op("vector", "tensor_mul", dsl, xsl, rbcL[:])
                        b.op("vector", "tensor_sub", dsl, dsl, mbcL[:], inc=("V", 1))
                        b.op("vector", "scalar_tensor_tensor", out=xsl, in0=dsl,
                             scalar=dD_sb[:, g * CG + ci:g * CG + ci + 1], in1=xsl,
                             op0=OP.mult, op1=OP.add, inc=("V", 1))
                        b.op("vector", "tensor_mul", dsl, dsl,
                             dt_sb[:, ci * TSM:(ci + 1) * TSM], inc=("V", 1))
                        yield
                    dtx_done = b.cnt["V"]
                    bc_g_last = {}
                    vbc_by_n = {}

                    def bcast_emit(n):
                        nb2 = n % 2
                        slot = n % 4
                        bcrv = bcr_sb[0:1, 0:2 * TSM]
                        s, v = bcr_free[0]
                        b.wait("sync", s, v)
                        b.wait("sync", "V", bc_ready)
                        b.op("sync", "dma_start",
                             out=bcrv.rearrange("a (k t) -> a k t", k=2),
                             in_=bc_sb[n:n + 17:16, :], inc=("BC", 16))
                        kb, kc = (4, 5) if nb2 == 0 else (6, 7)
                        for k2 in (kb, kc):
                            s, v = bank_free[k2]
                            b.wait("tensor", s, v)
                        b.dwait("tensor", "BC")
                        t_b = mm_group(kb, [(ones32[0:1, :], bcrv[0:1, 0:TSM])],
                                       out_ap=ps[kb][:, 0:TSM])
                        t_c = mm_group(kc, [(ones32[0:1, :], bcrv[0:1, TSM:2 * TSM])],
                                       out_ap=ps[kc][:, 0:TSM])
                        bcr_free[0] = ("T", t_c)
                        bsl = bbc_sb[:, slot * TSM:(slot + 1) * TSM]
                        csl = cbc_sb[:, slot * TSM:
                                     slot * TSM + TSM // 2].bitcast(BF16)
                        b.wait("vector", "T", t_c)
                        b.wait("vector", "G", bc_g_last.get(n - 4))
                        b.op("vector", "tensor_copy", out=bsl, in_=ps[kb][:, 0:TSM])
                        vbc = b.op("vector", "tensor_copy", out=csl,
                                   in_=ps[kc][:, 0:TSM], inc=("V", 1))
                        bank_free[kb] = ("V", vbc)
                        bank_free[kc] = ("V", vbc)
                        vbc_by_n[n] = vbc

                    def emit_st():
                        p, idx2 = ss["pend_st"].pop(0)
                        b.wait("scalar", "V", ss["vh"][p])
                        s_st = b.op("scalar", "copy", out=st_sb[:, idx2:idx2 + 1],
                                    in_=h_v[p % 4][:, TSM - 1:TSM], inc=("S", 1))
                        ss["hs"][p] = s_st

                    def emit_add():
                        p, cg2, xs2, ws2 = ss["pend_add"].pop(0)
                        if cg2 % 4 == 3:
                            a_i = b.op("vector", "tensor_add", xs2, xs2, ws2,
                                       inc=("V", 1))
                            ss["w"][p] = ("V", a_i)
                        else:
                            b.wait("gpsimd", "V", ss["vw"][p])
                            a_i = b.op("gpsimd", "tensor_add", xs2, xs2, ws2,
                                       inc=("G", 1))
                            ss["w"][p] = ("G", a_i)

                    for n in range(NST):
                        if n == 0:
                            bcast_emit(0)
                            bcast_emit(1)
                        vbc = vbc_by_n[n]
                        for ci in range(CG):
                            c_g = g * CG + ci
                            pos = ss["pos"]
                            r4 = pos % 4
                            absl, usl, hsl, wsl = ab_v[r4], u_v[r4], h_v[r4], w_vb[r4]
                            dslt = dt_sb[:, ci * TSM:(ci + 1) * TSM]
                            dslx = dtx_sb[:, ci * TSM:(ci + 1) * TSM]
                            xsl = xmf_sb[:, ci * TSM:(ci + 1) * TSM]
                            idx = n * DC + c_g
                            if pos >= 4:
                                b.wait("scalar", "V", ss["vh"].get(pos - 4))
                            if n == 0 and ci == 0:
                                b.wait("scalar", "V", dtx_done)
                            s_ab = b.op("scalar", "activation", out=absl, in_=dslt,
                                        func=AF.Exp,
                                        scale=a_sb[:, c_g * NST + n:c_g * NST + n + 1],
                                        inc=("S", 1))
                            if pos >= 4:
                                b.wait("gpsimd", "V", ss["vh"].get(pos - 4))
                            if ci == 0:
                                b.wait("gpsimd", "V", vbc)
                            g_u = b.op("gpsimd", "tensor_mul", usl, dslx,
                                       bbc_sb[:, (n % 4) * TSM:(n % 4 + 1) * TSM],
                                       inc=("G", 1))
                            b.wait("vector", "S", s_ab)
                            b.wait("vector", "G", g_u)
                            if pos >= 4:
                                b.wait("vector", "S", ss["hs"][pos - 4])
                                pe, pv2 = ss["w"][pos - 4]
                                if pe == "G":
                                    b.wait("vector", "G", pv2)
                            v_h = b.op("vector", "tensor_tensor_scan", out=hsl,
                                       data0=absl, data1=usl,
                                       initial=st_sb[:, idx:idx + 1],
                                       op0=OP.mult, op1=OP.add, inc=("V", 1))
                            v_w = b.op("vector", "tensor_mul", wsl, hsl,
                                       cbc_sb[:, (n % 4) * TSM:(n % 4) * TSM
                                              + TSM // 2].bitcast(BF16),
                                       inc=("V", 1))
                            ss["vh"][pos] = v_h
                            ss["vw"][pos] = v_w
                            ss["ab"][pos] = v_h
                            ss["u"][pos] = v_h
                            ss["pend_st"].append((pos, idx))
                            ss["pend_add"].append((pos, c_g, xsl, wsl))
                            if len(ss["pend_st"]) > 3:
                                emit_st()
                            if len(ss["pend_add"]) > 2:
                                emit_add()
                            ss["pos"] = pos + 1
                            yield
                        bc_g_last[n] = b.cnt["G"]
                        if n + 2 < NST:
                            bcast_emit(n + 2)
                    while ss["pend_st"]:
                        emit_st()
                    while ss["pend_add"]:
                        emit_add()
                    dt_guard[0] = b.cnt["S"]
                    fv = b.cnt["V"]
                    fg = b.cnt["G"]
                    b.wait("gpsimd", "V", fv)
                    b.op("gpsimd", "dma_start",
                         out=dram3(g * CG * P, CG * P, t0g, TSM),
                         in_=re3(xmf_sb[:], CG), inc=("XW", 16))
                    xm_free[0] = ("V", fv)
                    yield

        def run_weave(main_gen, aux_gen, ratio):
            if ratio is None:
                for _ in main_gen:
                    pass
                for _ in aux_gen:
                    pass
                return
            acc = 0.0
            main_alive = aux_alive = True
            while main_alive:
                try:
                    next(main_gen)
                except StopIteration:
                    main_alive = False
                acc += ratio
                while acc >= 1.0 and aux_alive:
                    try:
                        next(aux_gen)
                    except StopIteration:
                        aux_alive = False
                    acc -= 1.0
            while aux_alive:
                try:
                    next(aux_gen)
                except StopIteration:
                    aux_alive = False

        def drain(gen):
            for _ in gen:
                pass

        def spill_x(h, after_v):
            b.wait("scalar", "V", after_v)
            b.op("scalar", "dma_start",
                 out=xs_d[:, h * TH:(h + 1) * TH].rearrange("(c p) t -> p c t", p=P),
                 in_=re3(x_sb[:], DC), inc=("XS", 16))
            if debug:
                b.op("scalar", "dma_start",
                     out=dbg_d[:, h * TH:(h + 1) * TH].rearrange("(c p) t -> p c t", p=P),
                     in_=re3(x_sb[:], DC), inc=("XS", 16))

        def load_x_from(mk):
            b.dwait("scalar", "XS")
            b.dwait("scalar", "XO")
            b.dwait("scalar", "XW")
            b.op("scalar", "dma_start",
                 out=re3(x_sb[:, 0:8 * TH], 8), in_=mk(0), inc=("XL", 16))
            b.op("scalar", "dma_start",
                 out=re3(x_sb[:, 8 * TH:], 8), in_=mk(1), inc=("XL1", 16))
            b.dwait("vector", "XL")
            b.dwait("vector", "XL1")

        def store_out(h, after_v):
            b.wait("scalar", "V", after_v)
            b.op("scalar", "dma_start",
                 out=out_d[:, h * TH:(h + 1) * TH].rearrange("(c p) t -> p c t", p=P),
                 in_=re3(x_sb[:], DC), inc=("XO", 16))

        for eng in ("vector", "scalar", "tensor"):
            b.dwait(eng, "XL")
        v0 = 0
        for s_i in range(NSUB):
            v0 = ln_sub(s_i)
        q2_src[0] = q2_gen()
        for _ in range(70):
            next(q2_src[0])

        def q2_aux():
            while q2_src[0] is not None:
                try:
                    next(q2_src[0])
                except StopIteration:
                    q2_src[0] = None
                yield

        run_weave(ffn_gen(v0, jit=True), q2_aux(), 2.0)
        if debug:
            b.wait("scalar", "V", b.cnt["V"])
            b.op("scalar", "dma_start", out=dbg2_d, in_=sc_sb[:], inc=("XS", 16))
            b.op("scalar", "dma_start", out=dbg3_d, in_=srow[:], inc=("XS", 16))
        spill_x(0, b.cnt["V"])
        load_x_from(lambda q: x_d[q * 8 * P:(q + 1) * 8 * P,
                                  TH:2 * TH].rearrange("(c p) t -> p c t", p=P))
        for s_i in range(NSUB):
            v0 = ln_sub(s_i)
        run_weave(ffn_gen(v0), mixer_gen(0), WEAVE_RATIO)
        spill_x(1, b.cnt["V"])
        load_x_from(lambda q: dram3(q * 8 * P, 8 * P, 0, TH))
        for s_i in range(NSUB):
            v0 = ln_sub(s_i)
        run_weave(ffn_gen(v0), mixer_gen(1), WEAVE_RATIO)
        store_out(0, b.cnt["V"])
        load_x_from(lambda q: dram3(q * 8 * P, 8 * P, TH, TH))
        for s_i in range(NSUB):
            v0 = ln_sub(s_i)
        drain(ffn_gen(v0))
        store_out(1, b.cnt["V"])
        b.dwait("scalar", "XO")

        blk_ctx = es.enter_context(nc.Block())

        def replay(name):
            def f(q):
                for meth, args, kwargs, inc in b.ev[name]:
                    ins = getattr(q, meth)(*args, **kwargs)
                    if inc is not None:
                        ins.then_inc(*inc)
            return f

        blk_ctx.sync(replay("sync"))
        blk_ctx.gpsimd(replay("gpsimd"))
        blk_ctx.scalar(replay("scalar"))
        blk_ctx.vector(replay("vector"))
        blk_ctx.tensor(replay("tensor"))
    return nc


def kernel(L0_attn_x, L0_attn_A_log, L0_attn_D, L0_attn_W_xproj,
           L0_ffn_W_up, L0_ffn_W_gate, L0_ffn_W_down, stage="full"):
    xs, wmaps = host_layout(L0_attn_x, L0_attn_A_log, L0_attn_D, L0_attn_W_xproj,
                            L0_ffn_W_up, L0_ffn_W_gate, L0_ffn_W_down)
    nc = build_nc(stage)
    in_maps = [dict(x_fm=xs[b_], **wmaps) for b_ in range(8)]
    res = run_bass_kernel_spmd(nc, in_maps, core_ids=list(range(8)))
    out = np.stack([np.asarray(r["out_fm"]).reshape(D, T).T for r in res.results])
    return np.ascontiguousarray(out.astype(np.float32))
